# version 04
# baseline (speedup 1.0000x reference)
import numpy as np
import ml_dtypes
import concourse.bass as bass
import concourse.mybir as mybir
from concourse.bass_utils import run_bass_kernel_spmd

F32 = mybir.dt.float32
BF16 = mybir.dt.bfloat16
AF = mybir.ActivationFunctionType
ALU = mybir.AluOpType
AX = mybir.AxisListType

B, SEQ, H = 4, 4096, 1024
T = 2048
NCH = 8
NE = 9
EPS = 1e-5
TH = 1024


def build_nc():
    nc = bass.Bass("TRN2", target_bir_lowering=False)

    d_x = nc.dram_tensor("x_own", [H, T], F32, kind="ExternalInput").ap()
    d_xo = nc.dram_tensor("x_oth", [H, T], F32, kind="ExternalInput").ap()
    d_wq = nc.dram_tensor("wq", [128, 8 * H], F32, kind="ExternalInput").ap()
    d_wk = nc.dram_tensor("wk", [128, 8 * H], F32, kind="ExternalInput").ap()
    d_wv = nc.dram_tensor("wv", [128, 8 * H], F32, kind="ExternalInput").ap()
    d_wo = nc.dram_tensor("wout", [128, 8 * H], F32, kind="ExternalInput").ap()
    d_wg1 = nc.dram_tensor("wg1", [128, 8 * 64], F32, kind="ExternalInput").ap()
    d_wg2 = nc.dram_tensor("wg2", [64, H], F32, kind="ExternalInput").ap()
    d_wr = nc.dram_tensor("wr", [128, 8 * 8], F32, kind="ExternalInput").ap()
    d_wfc = nc.dram_tensor("wfc", [NE, 8, 128, 4096], F32, kind="ExternalInput").ap()
    d_wpj = nc.dram_tensor("wpj", [NE, 8, 128, 4096], F32, kind="ExternalInput").ap()
    d_bq = nc.dram_tensor("bq", [128, 8], F32, kind="ExternalInput").ap()
    d_bvr = nc.dram_tensor("bv_rep", [128, H], F32, kind="ExternalInput").ap()
    d_bg1 = nc.dram_tensor("bg1", [64, 1], F32, kind="ExternalInput").ap()
    d_bg2 = nc.dram_tensor("bg2", [128, 8], F32, kind="ExternalInput").ap()
    d_bo = nc.dram_tensor("bout", [128, 8], F32, kind="ExternalInput").ap()
    d_lng = nc.dram_tensor("lng", [128, 8], F32, kind="ExternalInput").ap()
    d_lnb = nc.dram_tensor("lnb", [128, 8], F32, kind="ExternalInput").ap()
    d_bfc = nc.dram_tensor("bfc_t", [128, NE * 32], F32, kind="ExternalInput").ap()
    d_bpj = nc.dram_tensor("bpj_t", [128, NE * 8], F32, kind="ExternalInput").ap()
    d_id = nc.dram_tensor("ident32", [128, 128], F32, kind="ExternalInput").ap()
    d_qm = nc.dram_tensor("qmask", [128, 128], BF16, kind="ExternalInput").ap()
    d_eps = nc.dram_tensor("epsv", [1, 1], F32, kind="ExternalInput").ap()
    d_gdram = nc.dram_tensor("gate_dram", [8, T], F32, kind="Internal").ap()
    d_out = nc.dram_tensor("out", [H, T], F32, kind="ExternalOutput").ap()

    BASE = 16512

    def A(name, shape, dtype, off):
        return nc.alloc_sbuf_tensor_at(name, shape, dtype, offset=BASE + off).ap()

    acc = A("acc", [128, NCH * T], F32, 0)
    xn = A("xn", [128, NCH * T], BF16, 65536)
    AR = 98304
    wk_sb = A("wk_sb", [128, 8 * H], BF16, AR)
    wv_sb = A("wv_sb", [128, 8 * H], BF16, AR + 16384)
    xo_sb = A("xo_sb", [128, NCH * 512], F32, AR + 32768)
    xno_sb = A("xno_sb", [128, NCH * 512], BF16, AR + 49152)
    wq_sb = A("wq_sb", [128, 8 * H], BF16, AR)
    wo_sb = A("wo_sb", [128, 8 * H], BF16, AR + 16384)
    q_sb = A("q_sb", [128, NCH * 512], BF16, AR + 32768)
    aol = A("aol", [128, NCH * 512], F32, AR + 40960)
    gated = A("gated", [128, NCH * 512], BF16, AR + 57344)
    h_sb = A("h_sb", [128, 32 * TH], BF16, AR)
    OUTS = 163840
    ring = [A(f"ring{i}", [128, 4096], BF16, OUTS + i * 8192) for i in range(2)]
    ek_tm = [A(f"ek{i}", [128, H], BF16, OUTS + i * 2048) for i in range(2)]
    VSW = 8 * 132
    v_sl = [A(f"vsl{i}", [128, VSW], BF16, OUTS + 4096 + i * 2816) for i in range(2)]
    gatec = A("gatec", [128, 512], F32, OUTS + 16384)
    gh_sb = A("gh_sb", [64, 512], BF16, OUTS + 18432)
    lgt = A("lgt", [8, T], F32, OUTS + 19456)
    gbc = [A(f"gbc{i}", [128, TH], F32, OUTS + 19456 + i * 4096) for i in range(2)]
    bvr = A("bvr", [128, H], F32, OUTS + 27648)
    SM = 196608
    kvbd = A("kvbd", [128, 8 * 128], BF16, SM)
    xsq = [A(f"xsq{i}", [128, 512], F32, SM + 2048 + i * 2048) for i in range(2)]
    ident = A("ident", [128, 128], F32, SM + 6144)
    qmask = A("qmask_sb", [128, 128], BF16, SM + 6656)
    bfc_sb = A("bfc_sb", [128, NE * 32], F32, SM + 6912)
    bpj_sb = A("bpj_sb", [128, NE * 8], F32, SM + 8064)
    wg1_sb = A("wg1_sb", [128, 8 * 64], BF16, SM + 8384)
    wg2_sb = A("wg2_sb", [64, H], BF16, SM + 9408)
    wr_sb = A("wr_sb", [128, 64], BF16, SM + 11456)
    rows = A("rows", [1, 2048], F32, OUTS + 19456)
    r_mu = rows[0:1, 0:512]
    r_ms = rows[0:1, 512:1024]
    r_sd = rows[0:1, 1024:1536]
    r_rs = rows[0:1, 1536:2048]
    gfs = [A(f"gfs{i}", [8, 128], F32, SM + 11584 + i * 512) for i in range(2)]
    onerow = A("onerow", [1, 128], F32, SM + 12608)
    onec = A("onec", [128, 1], F32, SM + 13120)
    ltm5 = A("ltm5", [128, 128], F32, SM + 13856)
    gtm5 = A("gtm5", [128, 128], F32, SM + 13856)
    e85 = A("e85", [128, 128], F32, SM + 14368)
    nmk5 = A("nmk5", [128, 128], F32, SM + 14880)
    p2m5 = A("p2m5", [128, 128], F32, SM + 15392)
    keep5 = A("keep5", [128, 128], F32, SM + 15392)
    nmx5 = A("nmx5", [128, 16], F32, SM + 15904)
    m1c5 = A("m1c5", [128, 16], F32, SM + 15968)
    m2c5 = A("m2c5", [128, 16], F32, SM + 16032)
    den5 = A("den5", [128, 16], F32, SM + 16096)
    rden5 = A("rden5", [128, 16], F32, SM + 16160)
    bq_sb = A("bq_sb", [128, 8], F32, SM + 13152)
    bg2_sb = A("bg2_sb", [128, 8], F32, SM + 13184)
    bo_sb = A("bo_sb", [128, 8], F32, SM + 13216)
    lng_sb = A("lng_sb", [128, 8], F32, SM + 13248)
    lnb_sb = A("lnb_sb", [128, 8], F32, SM + 13280)
    zs = A("zs", [128, 8], F32, SM + 13312)
    zr = A("zr", [128, 8], F32, SM + 13344)
    bg1_sb = A("bg1_sb", [64, 1], F32, SM + 13376)
    epsv = A("epsv_sb", [1, 1], F32, SM + 13408)
    ltm = A("ltm", [128, 8], F32, SM + 13440)
    p8 = A("p8", [128, 8], F32, SM + 13472)
    e8 = A("e8", [128, 8], F32, SM + 13504)
    msk1 = A("msk1", [128, 8], F32, SM + 13536)
    p2m = A("p2m", [128, 8], F32, SM + 13568)
    keep = A("keep", [128, 8], F32, SM + 13600)
    gtm = A("gtm", [128, 8], F32, SM + 13632)
    m1c = A("m1c", [128, 1], F32, SM + 13664)
    m2c = A("m2c", [128, 1], F32, SM + 13696)
    denc = A("denc", [128, 1], F32, SM + 13728)
    rden = A("rden", [128, 1], F32, SM + 13760)
    nmx = A("nmx", [128, 1], F32, SM + 13792)
    z8 = A("z8", [128, 1], F32, SM + 13824)
    rz8 = A("rz8", [128, 1], F32, SM + 13856)

    pb = [nc.alloc_psum_tensor(f"pb{i}", [128, 512], F32).ap() for i in range(8)]

    class Sem:
        def __init__(self, name):
            self.h = nc.alloc_semaphore(name)
            self.n = 0

    SD, SG, SP, SV, SA, SGC = (Sem(x) for x in ("SD", "SG", "SP", "SV", "SA", "SGC"))
    ops = []

    def op(ek, f, waits=(), inc=None):
        tgt = None
        if inc is not None:
            inc[0].n += inc[1]
            tgt = inc[0].n
        wl = tuple((s.h, v) for (s, v) in waits if v is not None)

        def run(eng, f=f, wl=wl, inc=inc):
            for hh, vv in wl:
                eng.wait_ge(hh, vv)
            ins = f(eng)
            if inc is not None:
                ins.then_inc(inc[0].h, inc[1])
        ops.append((ek, run))
        return tgt

    bank_free = [None] * 8

    def bw(*bs):
        return [bank_free[b] for b in bs if bank_free[b] is not None]

    for c in range(NCH):
        op("sync", lambda s, c=c: s.dma_start(acc[:, c * T:(c + 1) * T],
                                              d_x[c * 128:(c + 1) * 128, :]),
           inc=(SD, 16))
    for dst, src in [(bvr, d_bvr), (ident, d_id), (qmask, d_qm),
                     (bq_sb, d_bq), (bg2_sb, d_bg2), (bo_sb, d_bo),
                     (lng_sb, d_lng), (lnb_sb, d_lnb), (bg1_sb, d_bg1),
                     (bfc_sb, d_bfc), (bpj_sb, d_bpj), (epsv, d_eps)]:
        op("sync", lambda s, a=dst, b=src: s.dma_start(a, b), inc=(SD, 16))
    consts_loaded = SD.n
    for dst, src in [(wg1_sb, d_wg1), (wg2_sb, d_wg2), (wr_sb, d_wr),
                     (wk_sb, d_wk), (wv_sb, d_wv)]:
        op("gpsimd", lambda g, a=dst, b=src: g.dma_start(a, b), inc=(SG, 16))
    wkv_loaded = SG.n
    op("gpsimd", lambda g: g.memset(onec[:], 1.0), inc=(SGC, 1))
    op("gpsimd", lambda g: g.memset(onerow[:], 1.0), inc=(SGC, 1))
    for i in range(2):
        for cc in range(8):
            op("gpsimd", lambda g, i=i, cc=cc: g.memset(
                v_sl[i][:, cc * 132 + 128: cc * 132 + 129], 1.0), inc=(SGC, 1))
    init_done = SGC.n

    ln_guard = {"last": None}

    def emit_ln(src_fn, out_fn, bset, gamma_beta, ready):
        b_s1, b_s2, b_mu, b_r = bset
        sqd = [None, None]
        mm1 = mm2 = None
        for c in range(NCH):
            w = list(ready) if c == 0 else []
            if c == 0 and ln_guard["last"] is not None:
                w.append(ln_guard["last"])
            if sqd[c % 2] is not None:
                w.append((SP, sqd[c % 2]))
            sa = op("scalar", lambda s, c=c: s.activation(
                xsq[c % 2][:], src_fn(c), AF.Square), waits=w, inc=(SA, 1))
            w1 = (bw(b_s1) + list(ready) + [(SGC, init_done)]) if c == 0 else []
            mm1 = op("tensor", lambda t, c=c: t.matmul(
                pb[b_s1][0:1, :], onec[:], src_fn(c),
                start=(c == 0), stop=(c == NCH - 1)),
                waits=w1, inc=(SP, 1) if c == NCH - 1 else None)
            w2 = (bw(b_s2) if c == 0 else []) + [(SA, sa)]
            mm2 = op("tensor", lambda t, c=c: t.matmul(
                pb[b_s2][0:1, :], onec[:], xsq[c % 2][:],
                start=(c == 0), stop=(c == NCH - 1)),
                waits=w2, inc=(SP, 1))
            sqd[c % 2] = mm2
        v1 = op("vector", lambda v: v.tensor_scalar(
            r_mu[:], pb[b_s1][0:1, :], 1.0 / H, None, ALU.mult),
            waits=[(SP, mm2)], inc=(SV, 1))
        op("vector", lambda v: v.tensor_tensor(
            r_rs[:], r_mu[:], r_mu[:], ALU.mult), inc=(SV, 1))
        v3 = op("vector", lambda v: v.scalar_tensor_tensor(
            r_ms[:], pb[b_s2][0:1, :], 1.0 / H, r_rs[:], ALU.mult, ALU.subtract),
            inc=(SV, 1))
        a1 = op("scalar", lambda s: s.activation(
            r_sd[:], r_ms[:], AF.Sqrt, bias=epsv[0:1, 0:1]),
            waits=[(SV, v3)], inc=(SA, 1))
        v4 = op("vector", lambda v: v.reciprocal(r_rs[:], r_sd[:]),
                waits=[(SA, a1)], inc=(SV, 1))
        bank_free[b_s1] = (SV, v1)
        bank_free[b_s2] = (SV, v3)
        op("tensor", lambda t: t.matmul(pb[b_mu][:], onerow[:], r_mu[:],
                                        start=True, stop=True),
           waits=bw(b_mu) + [(SV, v1)], inc=(SP, 1))
        br = op("tensor", lambda t: t.matmul(pb[b_r][:], onerow[:], r_rs[:],
                                             start=True, stop=True),
               waits=bw(b_r) + [(SV, v4)], inc=(SP, 1))
        last = None
        last_mul = None
        for c in range(NCH):
            par = c % 2
            w = [(SP, br)] if c == 0 else []
            op("vector", lambda v, c=c, par=par: v.tensor_tensor(
                xsq[par][:], src_fn(c), pb[b_mu][:], ALU.subtract),
               waits=w, inc=(SV, 1))
            if gamma_beta is None:
                last_mul = op("vector", lambda v, c=c, par=par: v.tensor_tensor(
                    out_fn(c), xsq[par][:], pb[b_r][:], ALU.mult), inc=(SV, 1))
            else:
                op("vector", lambda v, c=c, par=par: v.tensor_tensor(
                    xsq[par][:], xsq[par][:], pb[b_r][:], ALU.mult), inc=(SV, 1))
                g_sb, bta = gamma_beta
                last_mul = op("vector", lambda v, c=c, par=par, g_sb=g_sb, bta=bta:
                              v.tensor_scalar(out_fn(c), xsq[par][:],
                                              g_sb[:, c:c + 1], bta[:, c:c + 1],
                                              ALU.mult, ALU.add), inc=(SV, 1))
            last = (SV, last_mul)
        bank_free[b_mu] = (SV, last_mul)
        bank_free[b_r] = (SV, last_mul)
        ln_guard["last"] = last
        return last

    tile_xn = []
    for tt in range(4):
        last = emit_ln(
            lambda c, tt=tt: acc[:, c * T + tt * 512: c * T + tt * 512 + 512],
            lambda c, tt=tt: xn[:, c * T + tt * 512: c * T + tt * 512 + 512],
            (4, 5, 6, 7), None, [(SD, consts_loaded)])
        tile_xn.append(last)
    xn_done = tile_xn[-1]

    st = {"first": True, "tb": 0, "last_s": None}
    ek_ready = [None, None]
    vs_ready = [None, None]
    ek_taken = [None, None]
    vs_taken = [None, None]

    def kv_block(src_fn, ready):
        par = st["tb"] % 2
        w0 = bw(0, 1, 2, 3) + list(ready)
        if st["tb"] == 0:
            w0 += [(SG, wkv_loaded)]
        last_mm = None
        for hc in range(NCH):
            op("tensor", lambda t, hc=hc: t.matmul(
                pb[0][:], src_fn(hc), wk_sb[:, hc * H: hc * H + 512],
                start=(hc == 0), stop=(hc == NCH - 1)),
               waits=w0 if hc == 0 else [], inc=None)
            op("tensor", lambda t, hc=hc: t.matmul(
                pb[1][:], src_fn(hc), wk_sb[:, hc * H + 512: hc * H + 1024],
                start=(hc == 0), stop=(hc == NCH - 1)), inc=None)
            op("tensor", lambda t, hc=hc: t.matmul(
                pb[2][:], src_fn(hc), wv_sb[:, hc * H: hc * H + 512],
                start=(hc == 0), stop=(hc == NCH - 1)), inc=None)
            last_mm = op("tensor", lambda t, hc=hc: t.matmul(
                pb[3][:], src_fn(hc), wv_sb[:, hc * H + 512: hc * H + 1024],
                start=(hc == 0), stop=(hc == NCH - 1)),
                inc=(SP, 1) if hc == NCH - 1 else None)
        we = [(SP, last_mm)]
        if ek_taken[par] is not None:
            we.append((SP, ek_taken[par]))
        op("scalar", lambda s, par=par: s.activation(
            ek_tm[par][:, 0:512], pb[0][:], AF.Exp), waits=we, inc=(SA, 1))
        a2 = op("scalar", lambda s, par=par: s.activation(
            ek_tm[par][:, 512:1024], pb[1][:], AF.Exp), inc=(SA, 1))
        ek_ready[par] = a2
        bank_free[0] = (SA, a2)
        bank_free[1] = (SA, a2)
        wv_ = [(SP, last_mm), (SGC, init_done)]
        if vs_taken[par] is not None:
            wv_.append((SP, vs_taken[par]))
        op("vector", lambda v, par=par: v.tensor_tensor(
            v_sl[par].rearrange("p (c w) -> p c w", w=132)[:, 0:4, 0:128],
            pb[2][:].rearrange("p (c w) -> p c w", w=128),
            bvr[:, 0:512].rearrange("p (c w) -> p c w", w=128), ALU.add),
            waits=wv_, inc=(SV, 1))
        v2 = op("vector", lambda v, par=par: v.tensor_tensor(
            v_sl[par].rearrange("p (c w) -> p c w", w=132)[:, 4:8, 0:128],
            pb[3][:].rearrange("p (c w) -> p c w", w=128),
            bvr[:, 512:1024].rearrange("p (c w) -> p c w", w=128), ALU.add),
            inc=(SV, 1))
        vs_ready[par] = v2
        bank_free[2] = (SV, v2)
        bank_free[3] = (SV, v2)
        sw = [(SA, ek_ready[par]), (SV, vs_ready[par])]
        if st["first"]:
            sw += bw(4, 5, 6, 7)
        smm = None
        s_start = st["first"]
        s_stop = (st["tb"] == 31)
        for c in range(NCH):
            bank = 4 + c // 2
            col = 256 * (c % 2)
            smm = op("tensor", lambda t, c=c, bank=bank, col=col, par=par,
                     s_start=s_start, s_stop=s_stop: t.matmul(
                pb[bank][:, col:col + 129],
                ek_tm[par][:, c * 128:(c + 1) * 128],
                v_sl[par][:, c * 132: c * 132 + 129],
                start=s_start, stop=(s_stop and c == NCH - 1)),
                waits=(sw if c == 0 else []),
                inc=(SP, 1) if c == NCH - 1 else None)
        ek_taken[par] = smm
        vs_taken[par] = smm
        st["first"] = False
        st["tb"] += 1
        st["last_s"] = smm

    for tt in range(4):
        for tb in range(4):
            base = tt * 512 + tb * 128
            kv_block(lambda hc, base=base: xn[:, hc * T + base: hc * T + base + 128],
                     [tile_xn[tt]])
    for s5 in range(4):
        dma_tgt = op("sync", lambda s, s5=s5: s.dma_start(
            xo_sb.rearrange("p (c w) -> p c w", w=512),
            d_xo.rearrange("(c p) t -> p c t", p=128)[:, :, s5 * 512:(s5 + 1) * 512]),
            waits=[st.get("xo_free")] if st.get("xo_free") else [],
            inc=(SD, 16))
        ln_last = emit_ln(lambda c: xo_sb[:, c * 512:(c + 1) * 512],
                          lambda c: xno_sb[:, c * 512:(c + 1) * 512],
                          (0, 1, 2, 3), None, [(SD, dma_tgt)])
        for tb in range(4):
            kv_block(lambda hc, tb=tb: xno_sb[:, hc * 512 + tb * 128:
                                              hc * 512 + tb * 128 + 128],
                     [ln_last])
        st["xo_free"] = (SP, st["last_s"])

    last_s = st["last_s"]

    az = None
    for c in range(NCH):
        bank = 4 + c // 2
        col = 256 * (c % 2) + 128
        az = op("scalar", lambda s, c=c, bank=bank, col=col: s.copy(
            zs[:, c:c + 1], pb[bank][:, col:col + 1]),
            waits=([(SP, last_s)] if c == 0 else []), inc=(SA, 1))
    vz = op("vector", lambda v: v.reciprocal(zr[:], zs[:]),
            waits=[(SA, az)], inc=(SV, 1))
    op("vector", lambda v: v.memset(nmx[:], 0.0), inc=(SV, 1))
    op("vector", lambda v: v.memset(z8[:], 0.0), inc=(SV, 1))
    vkv = None
    for c in range(NCH):
        bank = 4 + c // 2
        col = 256 * (c % 2)
        vkv = op("vector", lambda v, c=c, bank=bank, col=col: v.scalar_tensor_tensor(
            kvbd[:, c * 128:(c + 1) * 128], pb[bank][:, col:col + 128],
            zr[:, c:c + 1], qmask[:], ALU.mult, ALU.mult),
            waits=([(SV, vz)] if c == 0 else []), inc=(SV, 1))
    for b_ in range(4, 8):
        bank_free[b_] = (SV, vkv)

    op("gpsimd", lambda g: g.dma_start(wq_sb, d_wq),
       waits=[(SP, last_s)], inc=(SG, 16))
    wq_tgt = SG.n
    op("gpsimd", lambda g: g.dma_start(wo_sb, d_wo), inc=(SG, 16))
    wo_tgt = SG.n

    lg_evac = None
    for tt in range(4):
        w0 = bw(0) + [xn_done]
        mm = None
        for hc in range(NCH):
            mm = op("tensor", lambda t, hc=hc, tt=tt: t.matmul(
                pb[0][0:8, :], wr_sb[:, hc * 8:(hc + 1) * 8],
                xn[:, hc * T + tt * 512: hc * T + tt * 512 + 512],
                start=(hc == 0), stop=(hc == NCH - 1)),
                waits=w0 if hc == 0 else [],
                inc=(SP, 1) if hc == NCH - 1 else None)
        lg_evac = op("scalar", lambda s, tt=tt: s.copy(
            lgt[:, tt * 512:(tt + 1) * 512], pb[0][0:8, :]),
            waits=[(SP, mm)], inc=(SA, 1))
        bank_free[0] = (SA, lg_evac)
    ac_last = None
    for tb in range(16):
        bk = 1 + tb % 2
        tr1 = op("tensor", lambda t, tb=tb, bk=bk: t.matmul(
            pb[bk][0:128, 0:8], lgt[0:8, tb * 128:(tb + 1) * 128], ident[0:8, 0:8],
            start=True, stop=True),
            waits=bw(bk) + ([(SA, lg_evac)] if tb == 0 else []), inc=(SP, 1))
        ac_last = op("scalar", lambda s, tb=tb, bk=bk: s.copy(
            ltm5[:, tb * 8:(tb + 1) * 8], pb[bk][0:128, 0:8]),
            waits=[(SP, tr1)], inc=(SA, 1))
        bank_free[bk] = (SA, ac_last)
    vlast = None
    for tb in range(16):
        vlast = op("vector", lambda v, tb=tb: v.reduce_max(
            nmx5[:, tb:tb + 1], ltm5[:, tb * 8:(tb + 1) * 8], axis=AX.X,
            negate=True),
            waits=[(SA, ac_last)] if tb == 0 else [], inc=(SV, 1))
    alast = None
    for tb in range(16):
        alast = op("scalar", lambda s, tb=tb: s.activation(
            e85[:, tb * 8:(tb + 1) * 8], ltm5[:, tb * 8:(tb + 1) * 8], AF.Exp,
            bias=nmx5[:, tb:tb + 1]),
            waits=[(SV, vlast)] if tb == 0 else [], inc=(SA, 1))
    for tb in range(16):
        vlast = op("vector", lambda v, tb=tb: v.reduce_max(
            m1c5[:, tb:tb + 1], e85[:, tb * 8:(tb + 1) * 8], axis=AX.X),
            waits=[(SA, alast)] if tb == 0 else [], inc=(SV, 1))
    for tb in range(16):
        vlast = op("vector", lambda v, tb=tb: v.tensor_scalar(
            nmk5[:, tb * 8:(tb + 1) * 8], e85[:, tb * 8:(tb + 1) * 8],
            m1c5[:, tb:tb + 1], None, ALU.is_lt), inc=(SV, 1))
    for tb in range(16):
        vlast = op("vector", lambda v, tb=tb: v.tensor_tensor(
            p2m5[:, tb * 8:(tb + 1) * 8], e85[:, tb * 8:(tb + 1) * 8],
            nmk5[:, tb * 8:(tb + 1) * 8], ALU.mult), inc=(SV, 1))
    for tb in range(16):
        vlast = op("vector", lambda v, tb=tb: v.reduce_max(
            m2c5[:, tb:tb + 1], p2m5[:, tb * 8:(tb + 1) * 8], axis=AX.X),
            inc=(SV, 1))
    for tb in range(16):
        vlast = op("vector", lambda v, tb=tb: v.tensor_scalar(
            keep5[:, tb * 8:(tb + 1) * 8], e85[:, tb * 8:(tb + 1) * 8],
            m2c5[:, tb:tb + 1], None, ALU.is_ge), inc=(SV, 1))
    ad = op("gpsimd", lambda g: g.tensor_tensor(den5[:], m1c5[:], m2c5[:],
                                                ALU.add),
            waits=[(SV, vlast)], inc=(SGC, 1))
    vrd = op("vector", lambda v: v.reciprocal(rden5[:], den5[:]),
             waits=[(SGC, ad)], inc=(SV, 1))
    op("vector", lambda v: v.memset(nmx5[:, 0:1], 0.0), inc=(SV, 1))
    op("vector", lambda v: v.memset(nmx5[:, 1:2], 0.0), inc=(SV, 1))
    for tb in range(16):
        vlast = op("vector", lambda v, tb=tb: v.scalar_tensor_tensor(
            gtm5[:, tb * 8:(tb + 1) * 8], e85[:, tb * 8:(tb + 1) * 8],
            rden5[:, tb:tb + 1], keep5[:, tb * 8:(tb + 1) * 8],
            ALU.mult, ALU.mult), inc=(SV, 1))
    for tb in range(16):
        bk = 1 + tb % 2
        tr2 = op("tensor", lambda t, tb=tb, bk=bk: t.matmul(
            pb[bk][0:8, 0:128], gtm5[:, tb * 8:(tb + 1) * 8], ident[:, 0:128],
            start=True, stop=True),
            waits=bw(bk) + ([(SV, vlast)] if tb == 0 else []), inc=(SP, 1))
        gpar = tb % 2
        wge = [(SP, tr2)]
        if st.get(("gfs", gpar)) is not None:
            wge.append((SD, st[("gfs", gpar)]))
        g_evac = op("scalar", lambda s, gpar=gpar, bk=bk: s.copy(
            gfs[gpar][:], pb[bk][0:8, 0:128]), waits=wge, inc=(SA, 1))
        bank_free[bk] = (SA, g_evac)
        op("sync", lambda s, tb=tb, gpar=gpar: s.dma_start(
            d_gdram[:, tb * 128:(tb + 1) * 128], gfs[gpar][:]),
           waits=[(SA, g_evac)], inc=(SD, 16))
        st[("gfs", gpar)] = SD.n
    gate_dram_done = SD.n

    for tt in range(4):
        t0 = tt * 512
        qa = None
        for qf in range(NCH):
            bk = qf % 2
            w = bw(bk) + ([(SG, wq_tgt)] if tt == 0 and qf == 0 else [])
            mm = None
            for hc in range(NCH):
                mm = op("tensor", lambda t, hc=hc, qf=qf, bk=bk, t0=t0: t.matmul(
                    pb[bk][:], wq_sb[:, hc * H + qf * 128: hc * H + qf * 128 + 128],
                    xn[:, hc * T + t0: hc * T + t0 + 512],
                    start=(hc == 0), stop=(hc == NCH - 1)),
                    waits=w if hc == 0 else [],
                    inc=(SP, 1) if hc == NCH - 1 else None)
            qa = op("scalar", lambda s, qf=qf, bk=bk: s.activation(
                q_sb[:, qf * 512:(qf + 1) * 512], pb[bk][:], AF.Silu,
                bias=bq_sb[:, qf:qf + 1]), waits=[(SP, mm)], inc=(SA, 1))
            bank_free[bk] = (SA, qa)
        aoe = None
        for c in range(NCH):
            bk = 2 + c % 2
            w = bw(bk) + ([(SV, vkv), (SA, qa)] if c == 0 else [])
            mm = op("tensor", lambda t, c=c, bk=bk: t.matmul(
                pb[bk][:], kvbd[:, c * 128:(c + 1) * 128],
                q_sb[:, c * 512:(c + 1) * 512], start=True, stop=True),
                waits=w, inc=(SP, 1))
            aoe = op("vector", lambda v, c=c, bk=bk: v.tensor_copy(
                aol[:, c * 512:(c + 1) * 512], pb[bk][:]),
                waits=[(SP, mm)], inc=(SV, 1))
            bank_free[bk] = (SV, aoe)
        ln_last = emit_ln(lambda c: aol[:, c * 512:(c + 1) * 512],
                          lambda c: aol[:, c * 512:(c + 1) * 512],
                          (4, 5, 6, 7), (lng_sb, lnb_sb), [(SV, aoe)])
        w = bw(0)
        mm = None
        for hc in range(NCH):
            mm = op("tensor", lambda t, hc=hc, t0=t0: t.matmul(
                pb[0][0:64, :], wg1_sb[:, hc * 64:(hc + 1) * 64],
                xn[:, hc * T + t0: hc * T + t0 + 512],
                start=(hc == 0), stop=(hc == NCH - 1)),
                waits=w if hc == 0 else [],
                inc=(SP, 1) if hc == NCH - 1 else None)
        agh = op("vector", lambda v: v.tensor_scalar(
            gh_sb[:], pb[0][0:64, :], bg1_sb[0:64, 0:1], None, ALU.add),
            waits=[(SP, mm)], inc=(SV, 1))
        bank_free[0] = (SV, agh)
        vgt = None
        for oc in range(NCH):
            bk = 1 if oc % 2 == 0 else 0
            w = bw(bk) + ([(SV, agh)] if oc == 0 else [])
            mm = op("tensor", lambda t, oc=oc, bk=bk: t.matmul(
                pb[bk][:], wg2_sb[0:64, oc * 128:(oc + 1) * 128], gh_sb[:],
                start=True, stop=True), waits=w, inc=(SP, 1))
            asg = op("scalar", lambda s, oc=oc, bk=bk: s.activation(
                gatec[:], pb[bk][:], AF.Sigmoid, bias=bg2_sb[:, oc:oc + 1]),
                waits=[(SP, mm)] + ([(SV, vgt)] if vgt else []), inc=(SA, 1))
            vgt = op("vector", lambda v, oc=oc: v.tensor_tensor(
                gated[:, oc * 512:(oc + 1) * 512],
                aol[:, oc * 512:(oc + 1) * 512], gatec[:], ALU.mult),
                waits=[(SA, asg), ln_last], inc=(SV, 1))
            bank_free[bk] = (SV, vgt)
        for oc in range(NCH):
            bk = 2 + oc % 2
            w = bw(bk) + ([(SV, vgt)] if oc == 0 else []) \
                + ([(SG, wo_tgt)] if tt == 0 and oc == 0 else [])
            mm = None
            for c in range(NCH):
                mm = op("tensor", lambda t, c=c, oc=oc, bk=bk: t.matmul(
                    pb[bk][:], wo_sb[:, c * H + oc * 128: c * H + oc * 128 + 128],
                    gated[:, c * 512:(c + 1) * 512],
                    start=(c == 0), stop=(c == NCH - 1)),
                    waits=w if c == 0 else [],
                    inc=(SP, 1) if c == NCH - 1 else None)
            ve = op("vector", lambda v, oc=oc, bk=bk, t0=t0: v.scalar_tensor_tensor(
                acc[:, oc * T + t0: oc * T + t0 + 512], pb[bk][:],
                bo_sb[:, oc:oc + 1], acc[:, oc * T + t0: oc * T + t0 + 512],
                ALU.add, ALU.add), waits=[(SP, mm)], inc=(SV, 1))
            bank_free[bk] = (SV, ve)

    p4_done = SV.n

    iters = [(e, hf) for e in range(NE) for hf in range(2)]
    uses = []
    for (e, hf) in iters:
        for mg in range(8):
            uses.append(("fc", e, mg))
        for o in range(8):
            uses.append(("pj", e, o))
    ring_tgt = [None, None]
    ring_cons = [None, None]
    gbc_tgt = [None, None]
    gbc_cons = [None, None]
    blkctr = [0]

    def issue_blk_dma(kind, e, idx):
        r = blkctr[0] % 2
        w = [] if ring_cons[r] is None else [(SP, ring_cons[r])]
        src = d_wfc[e, idx] if kind == "fc" else d_wpj[e, idx]
        op("gpsimd", lambda g, src=src, r=r: g.dma_start(ring[r], src),
           waits=w, inc=(SG, 16))
        ring_tgt[r] = SG.n
        blkctr[0] += 1
        return r

    use_ring = {0: issue_blk_dma(uses[0][0], uses[0][1], uses[0][2])}
    ui = 0
    fcb_cons = [None, None]
    pjb_cons = [None, None]
    first6 = True
    for it_i, (e, hf) in enumerate(iters):
        if e >= 1:
            slot = it_i % 2
            w = [(SD, gate_dram_done), (SV, p4_done)]
            if gbc_cons[slot] is not None:
                w.append((SV, gbc_cons[slot]))
            op("sync", lambda s, e=e, hf=hf, slot=slot: s.dma_start(
                gbc[slot],
                d_gdram[e - 1:e, hf * TH:(hf + 1) * TH].broadcast_to([128, TH])),
                waits=w, inc=(SD, 16))
            gbc_tgt[slot] = SD.n
        last_gelu = None
        for mg in range(8):
            r = use_ring[ui]
            if ui + 1 < len(uses):
                nk = uses[ui + 1]
                use_ring[ui + 1] = issue_blk_dma(nk[0], nk[1], nk[2])
            for ml in range(4):
                m = mg * 4 + ml
                pp = m % 2
                bka, bkb = (0, 1) if pp == 0 else (2, 3)
                w = []
                if fcb_cons[pp] is not None:
                    w.append((SA, fcb_cons[pp]))
                if ml == 0:
                    w.append((SG, ring_tgt[r]))
                if first6 and m <= 1:
                    w += bw(bka, bkb)
                mm = None
                for k in range(NCH):
                    op("tensor", lambda t, k=k, ml=ml, r=r, bka=bka, hf=hf: t.matmul(
                        pb[bka][:],
                        ring[r][:, k * 512 + ml * 128: k * 512 + ml * 128 + 128],
                        xn[:, k * T + hf * TH: k * T + hf * TH + 512],
                        start=(k == 0), stop=(k == NCH - 1)),
                        waits=w if k == 0 else [], inc=None)
                    mm = op("tensor", lambda t, k=k, ml=ml, r=r, bkb=bkb, hf=hf: t.matmul(
                        pb[bkb][:],
                        ring[r][:, k * 512 + ml * 128: k * 512 + ml * 128 + 128],
                        xn[:, k * T + hf * TH + 512: k * T + hf * TH + 1024],
                        start=(k == 0), stop=(k == NCH - 1)),
                        inc=(SP, 1) if k == NCH - 1 else None)
                op("scalar", lambda s, m=m, bka=bka, e=e: s.activation(
                    h_sb[:, m * TH: m * TH + 512], pb[bka][:],
                    AF.Gelu_apprx_tanh, bias=bfc_sb[:, e * 32 + m: e * 32 + m + 1]),
                    waits=[(SP, mm)], inc=(SA, 1))
                g2_ = op("scalar", lambda s, m=m, bkb=bkb, e=e: s.activation(
                    h_sb[:, m * TH + 512: m * TH + 1024], pb[bkb][:],
                    AF.Gelu_apprx_tanh, bias=bfc_sb[:, e * 32 + m: e * 32 + m + 1]),
                    inc=(SA, 1))
                fcb_cons[pp] = g2_
                last_gelu = g2_
            ring_cons[r] = SP.n
            ui += 1
        for o in range(8):
            r = use_ring[ui]
            if ui + 1 < len(uses):
                nk = uses[ui + 1]
                use_ring[ui + 1] = issue_blk_dma(nk[0], nk[1], nk[2])
            pp = o % 2
            bka, bkb = (4, 5) if pp == 0 else (6, 7)
            w = [(SG, ring_tgt[r])]
            if pjb_cons[pp] is not None:
                w.append((SV, pjb_cons[pp]))
            if first6 and o <= 1:
                w += bw(bka, bkb)
            if o == 0:
                w.append((SA, last_gelu))
            mm = None
            for m in range(32):
                op("tensor", lambda t, m=m, r=r, bka=bka: t.matmul(
                    pb[bka][:], ring[r][:, m * 128:(m + 1) * 128],
                    h_sb[:, m * TH: m * TH + 512],
                    start=(m == 0), stop=(m == 31)),
                    waits=w if m == 0 else [], inc=None)
                mm = op("tensor", lambda t, m=m, r=r, bkb=bkb: t.matmul(
                    pb[bkb][:], ring[r][:, m * 128:(m + 1) * 128],
                    h_sb[:, m * TH + 512: m * TH + 1024],
                    start=(m == 0), stop=(m == 31)),
                    inc=(SP, 1) if m == 31 else None)
            ring_cons[r] = SP.n
            ui += 1
            o0 = o * T + hf * TH
            if e == 0:
                op("vector", lambda v, bka=bka, e=e, o=o, o0=o0: v.scalar_tensor_tensor(
                    acc[:, o0: o0 + 512], pb[bka][:],
                    bpj_sb[:, e * 8 + o: e * 8 + o + 1],
                    acc[:, o0: o0 + 512], ALU.add, ALU.add),
                    waits=[(SP, mm)], inc=(SV, 1))
                v2 = op("vector", lambda v, bkb=bkb, e=e, o=o, o0=o0: v.scalar_tensor_tensor(
                    acc[:, o0 + 512: o0 + 1024], pb[bkb][:],
                    bpj_sb[:, e * 8 + o: e * 8 + o + 1],
                    acc[:, o0 + 512: o0 + 1024], ALU.add, ALU.add), inc=(SV, 1))
            else:
                slot = it_i % 2
                op("vector", lambda v, bka=bka, e=e, o=o, slot=slot: v.scalar_tensor_tensor(
                    gatec[:], pb[bka][:], bpj_sb[:, e * 8 + o: e * 8 + o + 1],
                    gbc[slot][:, 0:512], ALU.add, ALU.mult),
                    waits=[(SP, mm), (SD, gbc_tgt[slot])], inc=(SV, 1))
                op("vector", lambda v, o0=o0: v.tensor_tensor(
                    acc[:, o0: o0 + 512], acc[:, o0: o0 + 512], gatec[:],
                    ALU.add), inc=(SV, 1))
                op("vector", lambda v, bkb=bkb, e=e, o=o, slot=slot: v.scalar_tensor_tensor(
                    gatec[:], pb[bkb][:], bpj_sb[:, e * 8 + o: e * 8 + o + 1],
                    gbc[slot][:, 512:1024], ALU.add, ALU.mult), inc=(SV, 1))
                v2 = op("vector", lambda v, o0=o0: v.tensor_tensor(
                    acc[:, o0 + 512: o0 + 1024], acc[:, o0 + 512: o0 + 1024],
                    gatec[:], ALU.add), inc=(SV, 1))
                gbc_cons[slot] = v2
            pjb_cons[pp] = v2
        first6 = False
    ffn_done = SV.n

    for c in range(NCH):
        op("sync", lambda s, c=c: s.dma_start(
            d_out[c * 128:(c + 1) * 128, :], acc[:, c * T:(c + 1) * T]),
           waits=[(SV, ffn_done)] if c == 0 else [], inc=(SD, 16))
    final_sd = SD.n
    op("sync", lambda s, v=final_sd: s.wait_ge(SD.h, v))

    with nc.Block() as block:
        @block.sync
        def _(eng):
            for ek, run in ops:
                if ek == "sync":
                    run(eng)

        @block.gpsimd
        def _(eng):
            for ek, run in ops:
                if ek == "gpsimd":
                    run(eng)

        @block.tensor
        def _(eng):
            for ek, run in ops:
                if ek == "tensor":
                    run(eng)

        @block.vector
        def _(eng):
            for ek, run in ops:
                if ek == "vector":
                    run(eng)

        @block.scalar
        def _(eng):
            for ek, run in ops:
                if ek == "scalar":
                    run(eng)

    return nc


def _tile_rows(w, nk):
    K, M = w.shape
    assert K == nk * 128
    return np.ascontiguousarray(
        w.reshape(nk, 128, M).transpose(1, 0, 2).reshape(128, nk * M))


def _bias_tile(b, n):
    return np.ascontiguousarray(b.reshape(n, 128).T)


def kernel(L0_attn_x, L0_attn_w_qkv, L0_attn_b_qkv, L0_attn_w_g1, L0_attn_b_g1,
           L0_attn_w_g2, L0_attn_b_g2, L0_attn_w_out, L0_attn_b_out,
           L0_attn_ln_g, L0_attn_ln_b, L0_ffn_Wfc_s, L0_ffn_bfc_s,
           L0_ffn_Wproj_s, L0_ffn_bproj_s, L0_ffn_Wr, L0_ffn_Wfc, L0_ffn_bfc,
           L0_ffn_Wproj, L0_ffn_bproj):
    f32 = np.float32
    x = np.asarray(L0_attn_x, dtype=f32)
    w_qkv = np.asarray(L0_attn_w_qkv, dtype=f32)
    b_qkv = np.asarray(L0_attn_b_qkv, dtype=f32)

    wq_t = _tile_rows(np.ascontiguousarray(w_qkv[0:H].T), 8)
    wk_t = _tile_rows(np.ascontiguousarray(w_qkv[H:2 * H].T), 8)
    wv_t = _tile_rows(np.ascontiguousarray(w_qkv[2 * H:3 * H].T), 8)
    wo_t = _tile_rows(np.ascontiguousarray(np.asarray(L0_attn_w_out, f32).T), 8)
    wg1_t = _tile_rows(np.ascontiguousarray(np.asarray(L0_attn_w_g1, f32).T), 8)
    wg2_t = np.ascontiguousarray(np.asarray(L0_attn_w_g2, f32).T)
    wr_t = _tile_rows(np.asarray(L0_ffn_Wr, f32), 8)

    wfc_all = np.concatenate([np.asarray(L0_ffn_Wfc_s, f32)[None],
                              np.asarray(L0_ffn_Wfc, f32)], axis=0)
    wpj_all = np.concatenate([np.asarray(L0_ffn_Wproj_s, f32)[None],
                              np.asarray(L0_ffn_Wproj, f32)], axis=0)
    wfc_h = np.ascontiguousarray(
        wfc_all.reshape(NE, 8, 128, 8, 512).transpose(0, 3, 2, 1, 4)
        .reshape(NE, 8, 128, 4096))
    wpj_h = np.ascontiguousarray(
        wpj_all.reshape(NE, 32, 128, 8, 128).transpose(0, 3, 2, 1, 4)
        .reshape(NE, 8, 128, 4096))

    bfc_all = np.concatenate([np.asarray(L0_ffn_bfc_s, f32)[None],
                              np.asarray(L0_ffn_bfc, f32)], axis=0)
    bpj_all = np.concatenate([np.asarray(L0_ffn_bproj_s, f32)[None],
                              np.asarray(L0_ffn_bproj, f32)], axis=0)
    bfc_t = np.ascontiguousarray(
        np.stack([_bias_tile(bfc_all[e], 32) for e in range(NE)], axis=1)
        .reshape(128, NE * 32))
    bpj_t = np.ascontiguousarray(
        np.stack([_bias_tile(bpj_all[e], 8) for e in range(NE)], axis=1)
        .reshape(128, NE * 8))

    qmask = np.kron(np.eye(2, dtype=f32), np.ones((64, 64), f32))
    common = dict(
        wq=wq_t, wk=wk_t, wv=wv_t, wout=wo_t, wg1=wg1_t, wg2=wg2_t, wr=wr_t,
        wfc=wfc_h, wpj=wpj_h,
        bq=_bias_tile(b_qkv[0:H], 8),
        bv_rep=np.ascontiguousarray(np.tile(b_qkv[2 * H:3 * H][None, :], (128, 1))),
        bg1=np.ascontiguousarray(np.asarray(L0_attn_b_g1, f32)[:, None]),
        bg2=_bias_tile(np.asarray(L0_attn_b_g2, f32), 8),
        bout=_bias_tile(np.asarray(L0_attn_b_out, f32), 8),
        lng=_bias_tile(np.asarray(L0_attn_ln_g, f32), 8),
        lnb=_bias_tile(np.asarray(L0_attn_ln_b, f32), 8),
        bfc_t=bfc_t, bpj_t=bpj_t,
        ident32=np.eye(128, dtype=f32),
        qmask=qmask.astype(ml_dtypes.bfloat16),
        epsv=np.full((1, 1), EPS, dtype=f32),
    )

    in_maps = []
    for core in range(8):
        b, hf = core // 2, core % 2
        m = dict(common)
        m["x_own"] = np.ascontiguousarray(x[b, hf * T:(hf + 1) * T].T)
        m["x_oth"] = np.ascontiguousarray(x[b, (1 - hf) * T:(2 - hf) * T].T)
        in_maps.append(m)

    nc = build_nc()
    res = run_bass_kernel_spmd(nc, in_maps, core_ids=list(range(8)))

    out = np.empty((B, SEQ, H), dtype=f32)
    for core in range(8):
        b, hf = core // 2, core % 2
        out[b, hf * T:(hf + 1) * T] = res.results[core]["out"].T
    return out
